# version 23
# baseline (speedup 1.0000x reference)
import contextlib
import numpy as np
import ml_dtypes
import concourse.bass as bass
import concourse.mybir as mybir
from concourse.alu_op_type import AluOpType as OP
from concourse.bass_utils import run_bass_kernel_spmd

F32 = mybir.dt.float32
BF16 = mybir.dt.bfloat16
AF = mybir.ActivationFunctionType
AX = mybir.AxisListType

N, D, H, DH = 4096, 1024, 16, 64
NT = N // 128
FB = D // 128
S_SCALE = float((DH ** 0.5) ** (-0.25))
EPS = 1e-5


def build_nc():
    nc = bass.Bass("TRN2", target_bir_lowering=False)
    x_h = nc.dram_tensor("x", [N, D], F32, kind="ExternalInput").ap()
    wq_h = nc.dram_tensor("wq", [D, D], F32, kind="ExternalInput").ap()
    wv_h = nc.dram_tensor("wv", [D, D], F32, kind="ExternalInput").ap()
    wo_h = nc.dram_tensor("wo", [D, D], F32, kind="ExternalInput").ap()
    fw_h = nc.dram_tensor("fw", [D, D], F32, kind="ExternalInput").ap()
    bq_h = nc.dram_tensor("bq", [D], F32, kind="ExternalInput").ap()
    bv_h = nc.dram_tensor("bv", [D], F32, kind="ExternalInput").ap()
    bo_h = nc.dram_tensor("bo", [D], F32, kind="ExternalInput").ap()
    fsc_h = nc.dram_tensor("fsc", [D], F32, kind="ExternalInput").ap()
    fbi_h = nc.dram_tensor("fbi", [D], F32, kind="ExternalInput").ap()
    id_h = nc.dram_tensor("ident", [128, 128], BF16, kind="ExternalInput").ap()
    out_h = nc.dram_tensor("out", [N, D], F32, kind="ExternalOutput").ap()

    es = contextlib.ExitStack()
    with es:
        sb = lambda name, shape, dt=F32: es.enter_context(nc.sbuf_tensor(name, shape, dt))
        s_wo = sb("s_wo", [128, FB * D], BF16)
        s_fw = sb("s_fw", [128, FB * D], BF16)
        s_qsT = sb("s_qsT", [128, FB * N], BF16)
        s_ctx = sb("s_ctx", [128, D], BF16)
        s_bq = sb("s_bq", [128, D])
        s_sbv = sb("s_sbv", [128, D])
        s_fsc = sb("s_fsc", [128, D])
        s_bo = sb("s_bo", [128, D], BF16)
        s_one = sb("s_one", [1, 128], BF16)
        s_fbi_row = sb("s_fbi_row", [1, D], BF16)
        s_id = sb("s_id", [128, 128], BF16)
        s_m = sb("s_m", [128, FB])
        s_cst = sb("s_cst", [128, 4])
        s_tgp = sb("s_tgp", [128, 1])
        p_tr = es.enter_context(nc.psum_tensor("p_tr", [128, 1024], BF16))
        p_a = es.enter_context(nc.psum_tensor("p_a", [128, 1024], F32))
        p_b = es.enter_context(nc.psum_tensor("p_b", [128, 1024], F32))
        p_c = es.enter_context(nc.psum_tensor("p_c", [128, 1536], F32))

        es_a = contextlib.ExitStack()
        sba = lambda name, shape, dt=F32: es_a.enter_context(nc.sbuf_tensor(name, shape, dt))
        s_wq = sba("s_wq", [128, FB * D], BF16)
        s_wv = sba("s_wv", [128, FB * D], BF16)
        s_xb = [sba(f"s_xb{i}", [128, D], BF16) for i in range(5)]
        s_xT = [sba(f"s_xT{i}", [128, D], BF16) for i in range(2)]
        s_qf = [sba(f"s_qf{i}", [128, D]) for i in range(2)]
        s_ef = [sba(f"s_ef{i}", [128, D]) for i in range(2)]
        s_ss = [sba(f"s_ss{i}", [128, H]) for i in range(2)]
        s_sr = [sba(f"s_sr{i}", [128, H]) for i in range(2)]
        s_qsb = [sba(f"s_qsb{i}", [128, D], BF16) for i in range(2)]
        s_ke = [sba(f"s_ke{i}", [128, D], BF16) for i in range(3)]
        s_vb = [sba(f"s_vb{i}", [128, FB * 129], BF16) for i in range(3)]
        s_fqb = [sba(f"s_fqb{i}", [128, D], BF16) for i in range(5)]
        s_fsg = [sba(f"s_fsg{i}", [128, D], BF16) for i in range(2)]
        es_a.close()

        es_b = contextlib.ExitStack()
        sbb = lambda name, shape, dt=F32: es_b.enter_context(nc.sbuf_tensor(name, shape, dt))
        s_xr = [sbb(f"s_xr{i}", [128, D]) for i in range(5)]
        s_zb = [sbb(f"s_zb{i}", [128, D]) for i in range(3)]
        s_aT = [sbb(f"s_aT{i}", [128, D], BF16) for i in range(2)]
        s_z1 = [sbb(f"s_z1{i}", [128, D]) for i in range(2)]
        s_y1b = [sbb(f"s_y1b{i}", [128, D], BF16) for i in range(5)]
        s_y1T = [sbb(f"s_y1T{i}", [128, D], BF16) for i in range(2)]
        s_yt = [sbb(f"s_yt{i}", [128, D]) for i in range(2)]
        s_yb = [sbb(f"s_yb{i}", [128, D]) for i in range(3)]
        s_of = [sbb(f"s_of{i}", [128, D]) for i in range(3)]
        s_st = [sbb(f"s_st{i}", [128, 40]) for i in range(2)]

        sems = {}
        for i in range(5):
            sems[f"dxb{i}"] = es.enter_context(nc.semaphore(f"dxb{i}"))
        for i in range(5):
            sems[f"dxr{i}"] = es.enter_context(nc.semaphore(f"dxr{i}"))
        for i in range(FB):
            sems[f"dwq{i}"] = es.enter_context(nc.semaphore(f"dwq{i}"))
            sems[f"dwv{i}"] = es.enter_context(nc.semaphore(f"dwv{i}"))
        for i in range(5):
            sems[f"dfw{i}"] = es.enter_context(nc.semaphore(f"dfw{i}"))
        for i in range(3):
            sems[f"dout{i}"] = es.enter_context(nc.semaphore(f"dout{i}"))
        for i in range(2):
            sems[f"dtx{i}"] = es.enter_context(nc.semaphore(f"dtx{i}"))
        for name in ["dwo", "did", "dbo", "dbq",
                     "dbi", "dwt", "dfw", "dxb", "dxr", "dout",
                     "dv_pre", "dv_fgs", "pe_fw", "dv_fwe",
                     "pe_xt", "dv_xt", "pe_q", "dv_q", "ac_e", "dv_qs",
                     "ac_ke", "pe_qt", "dv_qt", "pe_v", "dv_v", "pe_cx",
                     "dv_ctx",
                     "pe_at", "ac_aT", "pe_o", "dv_z1", "dv_bn1", "ac_r1",
                     "dv_nm1", "ac_y1", "pe_y1t", "dv_y1T", "pe_f", "dv_y2",
                     "gp_zb", "gp_yc", "dv_bn2", "ac_r2", "dv_nm2", "ac_out", "drow",
                     "dv_sm", "ac_sm"]:
            sems[name] = es.enter_context(nc.semaphore(name))
        g = type("G", (), sems)
        dxb_s = [sems[f"dxb{i}"] for i in range(5)]
        dxr_s = [sems[f"dxr{i}"] for i in range(5)]
        dwq_s = [sems[f"dwq{i}"] for i in range(FB)]
        dwv_s = [sems[f"dwv{i}"] for i in range(FB)]
        dfw_s = [sems[f"dfw{i}"] for i in range(5)]
        dout_s = [sems[f"dout{i}"] for i in range(3)]
        dtx_s = [sems[f"dtx{i}"] for i in range(2)]

        def W(eng, *pairs):
            for sem, v in pairs:
                if v >= 1:
                    eng.wait_ge(sem, v)

        _smc = {"dv": 0, "ac": 0}
        def chain(eng, ins, which):
            sem = g.dv_sm if which == "dv" else g.ac_sm
            _smc[which] += 1
            ins.then_inc(sem, 1)
            eng.wait_ge(sem, _smc[which])

        qsT3 = s_qsT[:, :].rearrange("p (f n) -> p f n", f=FB)
        p_pa_bf = p_a[:, :].bitcast(BF16)[:, 0:1024]
        fw3 = s_fw[:, :].rearrange("p (ib n) -> p ib n", ib=FB)
        ptr3 = p_tr[:, :].rearrange("p (f n) -> p f n", f=FB)
        vb3 = [s_vb[i][:, :].rearrange("p (h c) -> p h c", h=FB) for i in range(3)]
        pb3 = p_b[:, :].rearrange("p (h c) -> p h c", h=FB)

        def ctx_off(hp):
            return (hp // 3) * 512 + (hp % 3) * 129

        def body_init(eng):
            for s in sems.values():
                eng.sem_clear(s)

        def body_sync(eng):
            for c in range(NT):
                W(eng, (dxb_s[c % 5], 16 * (c // 5 + 1)),
                  (g.pe_q, c - 1), (g.pe_v, c - 1))
                for f in range(5):
                    eng.dma_start(s_xT[c % 2][:, f * 128:(f + 1) * 128],
                                  s_xb[c % 5][:, f * 128:(f + 1) * 128],
                                  transpose=True).then_inc(dtx_s[c % 2], 16)
            for j in range(NT):
                W(eng, (g.ac_out, j + 2))
                eng.dma_start(out_h[j * 128:(j + 1) * 128, :], s_of[j % 3][:, :]
                              ).then_inc(dout_s[j % 3], 16)
            for s in range(3):
                eng.wait_ge(dout_s[s], 16 * len([j for j in range(NT) if j % 3 == s]))

        def body_gp(eng):
            eng.dma_start(s_id[:, :], id_h).then_inc(g.did, 16)
            for j in range(2):
                eng.dma_start(s_xb[j][:, :], x_h[j * 128:(j + 1) * 128, :]
                              ).then_inc(dxb_s[j], 16)
            eng.dma_start(s_bq[:, :], bq_h.unsqueeze(0).broadcast_to([128, D])
                          ).then_inc(g.dbq, 16)
            for dst, src in [(s_sbv, bv_h), (s_fsc, fsc_h)]:
                eng.dma_start(dst[:, :], src.unsqueeze(0).broadcast_to([128, D])
                              ).then_inc(g.dbi, 16)
            eng.dma_start(s_fbi_row[:, :], fbi_h.unsqueeze(0)).then_inc(g.drow, 16)
            eng.dma_start(s_bo[:, :], bo_h.unsqueeze(0).broadcast_to([128, D])
                          ).then_inc(g.dbo, 16)
            for kb in range(FB):
                eng.dma_start(s_wq[:, kb * D:(kb + 1) * D],
                              wq_h[kb * 128:(kb + 1) * 128, :]
                              ).then_inc(dwq_s[kb], 16)
            for j in range(2, 4):
                eng.dma_start(s_xb[j][:, :], x_h[j * 128:(j + 1) * 128, :]
                              ).then_inc(dxb_s[j], 16)
            for kb in range(FB):
                eng.dma_start(s_wv[:, kb * D:(kb + 1) * D],
                              wv_h[kb * 128:(kb + 1) * 128, :]
                              ).then_inc(dwv_s[kb], 16)
            eng.dma_start(s_wo[:, :].rearrange("p (kb n) -> p kb n", kb=FB),
                          wo_h.rearrange("(kb p) n -> p kb n", p=128)
                          ).then_inc(g.dwo, 16)
            eng.dma_start(s_xb[4][:, :], x_h[4 * 128:5 * 128, :]
                          ).then_inc(dxb_s[4], 16)
            for j in range(5, NT):
                W(eng, (dtx_s[(j - 5) % 2], 128 * ((j - 5) // 2 + 1)))
                eng.dma_start(s_xb[j % 5][:, :], x_h[j * 128:(j + 1) * 128, :]
                              ).then_inc(dxb_s[j % 5], 16)
                if j == 8:
                    for ob in range(3):
                        eng.dma_start(s_fqb[ob][:, :],
                                      fw_h[ob * 128:(ob + 1) * 128, :]
                                      ).then_inc(dfw_s[ob], 16)
                if j == 10:
                    for ob in range(3, 5):
                        eng.dma_start(s_fqb[ob][:, :],
                                      fw_h[ob * 128:(ob + 1) * 128, :]
                                      ).then_inc(dfw_s[ob], 16)
            for ob in range(5, FB):
                W(eng, (g.dv_fgs, ob - 4))
                eng.dma_start(s_fqb[ob % 5][:, :],
                              fw_h[ob * 128:(ob + 1) * 128, :]
                              ).then_inc(dfw_s[ob % 5], 16)
            W(eng, (g.dv_ctx, 1))
            for j in range(4):
                eng.dma_start(s_xr[j][:, :], x_h[j * 128:(j + 1) * 128, :]
                              ).then_inc(dxr_s[j], 16)
            for k in range(NT):
                W(eng, (dxr_s[k % 5], 16 * (k // 5 + 1)), (g.dv_z1, k - 2))
                if k == 0:
                    W(eng, (g.dbo, 16))
                eng.tensor_tensor(s_zb[k % 3][:, :], s_xr[k % 5][:, :],
                                  s_bo[:, :], OP.add).then_inc(g.gp_zb, 1)
                if k + 4 < NT:
                    W(eng, (g.gp_zb, k))
                    eng.dma_start(s_xr[(k + 4) % 5][:, :],
                                  x_h[(k + 4) * 128:(k + 5) * 128, :]
                                  ).then_inc(dxr_s[(k + 4) % 5], 16)

        def body_pe(eng):
            W(eng, (g.did, 16))
            for k in range(NT + 2):
                if k >= 2:
                    c = k - 2
                    W(eng, (g.ac_ke, k - 1), (g.dv_v, k - 1))
                    if c == 0:
                        W(eng, (g.dv_pre, 2))
                    for hp in range(FB):
                        off = ctx_off(hp)
                        ins = eng.matmul(p_c[:, off:off + 129],
                                         s_ke[c % 3][:, hp * 128:(hp + 1) * 128],
                                         s_vb[c % 3][:, hp * 129:(hp + 1) * 129],
                                         start=(c == 0 and hp % 3 == 0),
                                         stop=(c == NT - 1 and hp in (2, 5, 7)),
                                         skip_group_check=True)
                    ins.then_inc(g.pe_cx, 1)
                if k < NT:
                    W(eng, (dtx_s[k % 2], 128 * (k // 2 + 1)), (g.dv_q, k))
                    for kb in range(FB):
                        if k == 0:
                            W(eng, (dwq_s[kb], 16))
                        for hh in range(2):
                            ins = eng.matmul(
                                p_a[:, hh * 512:(hh + 1) * 512],
                                s_xT[k % 2][:, kb * 128:(kb + 1) * 128],
                                s_wq[:, kb * D + hh * 512: kb * D + (hh + 1) * 512],
                                start=(kb == 0), stop=(kb == FB - 1))
                    ins.then_inc(g.pe_q, 1)
                if k >= 2:
                    c = k - 2
                    W(eng, (g.dv_qs, k - 1), (g.dv_qt, k - 2))
                    for f in range(FB):
                        ins = eng.transpose(p_tr[:, f * 128:(f + 1) * 128],
                                            s_qsb[c % 2][:, f * 128:(f + 1) * 128],
                                            s_id[:, :])
                    ins.then_inc(g.pe_qt, 1)
                if k < NT:
                    W(eng, (g.dv_v, k))
                    for kb in range(FB):
                        if k == 0:
                            W(eng, (dwv_s[kb], 16))
                        for hh in range(2):
                            ins = eng.matmul(
                                p_b[:, hh * 512:(hh + 1) * 512],
                                s_xT[k % 2][:, kb * 128:(kb + 1) * 128],
                                s_wv[:, kb * D + hh * 512: kb * D + (hh + 1) * 512],
                                start=(kb == 0), stop=(kb == FB - 1))
                    ins.then_inc(g.pe_v, 1)
            for ob in range(FB):
                W(eng, (g.dv_fgs, ob + 1), (g.dv_fwe, ob - 1))
                if ob == 0:
                    W(eng, (g.dv_qt, NT))
                if ob == 1:
                    W(eng, (g.dv_q, NT))
                bank = p_tr if ob % 2 == 0 else p_pa_bf
                for ib in range(FB):
                    ins = eng.transpose(bank[:, ib * 128:(ib + 1) * 128],
                                        s_fsg[ob % 2][:, ib * 128:(ib + 1) * 128],
                                        s_id[:, :])
                ins.then_inc(g.pe_fw, 1)
            for k in range(NT + 7):
                if k < NT:
                    W(eng, (g.ac_aT, k))
                    if k == 0:
                        W(eng, (g.dv_ctx, 1), (g.dv_fwe, FB))
                    for hp in range(FB):
                        ins = eng.matmul(
                            p_a[:, hp * 128:(hp + 1) * 128],
                            s_ctx[:, hp * 128:(hp + 1) * 128],
                            s_qsT[:, hp * N + k * 128: hp * N + (k + 1) * 128],
                            start=True, stop=True, skip_group_check=True)
                    ins.then_inc(g.pe_at, 1)
                if 4 <= k < NT + 4:
                    c = k - 4
                    W(eng, (g.ac_y1, k - 3), (g.dv_y1T, k - 4))
                    if k == 4:
                        W(eng, (g.dv_fwe, FB))
                    for f in range(FB):
                        ins = eng.transpose(p_tr[:, f * 128:(f + 1) * 128],
                                            s_y1b[c % 5][:, f * 128:(f + 1) * 128],
                                            s_id[:, :])
                    ins.then_inc(g.pe_y1t, 1)
                if 5 <= k < NT + 5:
                    c = k - 5
                    W(eng, (g.dv_y1T, k - 4), (g.dv_y2, k - 5))
                    if k == 5:
                        W(eng, (g.dv_ctx, 1), (g.dv_fwe, FB), (g.drow, 16),
                          (g.dv_pre, 2))
                    for kb in range(FB):
                        for hh in range(2):
                            ins = eng.matmul(
                                p_c[:, hh * 512:(hh + 1) * 512],
                                s_y1T[c % 2][:, kb * 128:(kb + 1) * 128],
                                s_fw[:, kb * D + hh * 512: kb * D + (hh + 1) * 512],
                                start=(kb == 0), stop=False)
                    for hh in range(2):
                        ins = eng.matmul(p_c[:, hh * 512:(hh + 1) * 512],
                                         s_one[0:1, 0:128],
                                         s_fbi_row[0:1, hh * 512:(hh + 1) * 512],
                                         start=False, stop=True)
                    ins.then_inc(g.pe_f, 1)
                if 1 <= k < NT + 1:
                    c = k - 1
                    W(eng, (g.ac_aT, k), (g.dv_z1, k - 1))
                    if k == 1:
                        W(eng, (g.dwo, 16), (g.dv_v, NT),
                          (g.drow, 16), (g.dv_pre, 2))
                    for kb in range(FB):
                        for hh in range(2):
                            ins = eng.matmul(
                                p_b[:, hh * 512:(hh + 1) * 512],
                                s_aT[c % 2][:, kb * 128:(kb + 1) * 128],
                                s_wo[:, kb * D + hh * 512: kb * D + (hh + 1) * 512],
                                start=(kb == 0), stop=(kb == FB - 1))
                    ins.then_inc(g.pe_o, 1)

        def body_dv(eng):
            eng.memset(s_cst[:, 0:1], EPS)
            eng.memset(s_cst[:, 1:2], -0.5)
            eng.memset(vb3[0][:, :, 128:129], 1.0)
            eng.memset(vb3[1][:, :, 128:129], 1.0)
            eng.memset(vb3[2][:, :, 128:129], 1.0)
            eng.memset(s_one[:, :], 1.0)
            W(eng, (g.dbi, 32))
            eng.tensor_scalar(s_sbv[:, :], s_sbv[:, :], S_SCALE, None, OP.mult
                              ).then_inc(g.dv_pre, 1)
            eng.tensor_scalar(s_fsc[:, :], s_fsc[:, :], 2.0, None, OP.mult
                              ).then_inc(g.dv_pre, 1)

            for k in range(NT + 2):
                if 1 <= k <= NT:
                    c = k - 1
                    W(eng, (g.ac_e, k))
                    chain(eng, eng.tensor_reduce(
                        s_ss[c % 2][:, :],
                        s_ef[c % 2][:, :].rearrange("p (h w) -> p h w", h=H),
                        AX.X, OP.add), "dv")
                    chain(eng, eng.reciprocal(s_sr[c % 2][:, :], s_ss[c % 2][:, :]),
                          "dv")
                    W(eng, (g.ac_ke, k - 2), (g.pe_qt, k - 2))
                    eng.scalar_tensor_tensor(
                        s_qsb[c % 2][:, :].rearrange("p (h w) -> p h w", h=H),
                        s_ef[c % 2][:, :].rearrange("p (h w) -> p h w", h=H),
                        S_SCALE,
                        s_sr[c % 2][:, :].unsqueeze(2).broadcast_to([128, H, DH]),
                        OP.mult, OP.mult).then_inc(g.dv_qs, 1)
                if k < NT:
                    W(eng, (g.pe_q, k + 1), (g.ac_e, k - 1))
                    if k == 0:
                        W(eng, (g.dbq, 16))
                    for hh in range(2):
                        ins = eng.scalar_tensor_tensor(
                            s_qf[k % 2][:, hh * 512:(hh + 1) * 512],
                            p_a[:, hh * 512:(hh + 1) * 512],
                            1.0, s_bq[:, hh * 512:(hh + 1) * 512],
                            OP.mult, OP.add)
                    ins.then_inc(g.dv_q, 1)
                if k >= 2:
                    c = k - 2
                    W(eng, (g.pe_qt, k - 1))
                    eng.tensor_copy(qsT3[:, :, c * 128:(c + 1) * 128],
                                    ptr3[:, :, :]).then_inc(g.dv_qt, 1)
                if k < NT:
                    W(eng, (g.pe_v, k + 1), (g.pe_cx, k - 2))
                    eng.tensor_copy(vb3[k % 3][:, :, 0:128], pb3[:, :, :]).then_inc(g.dv_v, 1)
            for ob in range(FB):
                W(eng, (dfw_s[ob % 5], 16 * (ob // 5 + 1)))
                W(eng, (g.pe_fw, ob - 1))
                eng.tensor_scalar(s_fsg[ob % 2][:, :], s_fqb[ob % 5][:, :],
                                  0.0, -0.5, OP.is_ge, OP.add
                                  ).then_inc(g.dv_fgs, 1)
                if ob >= 1:
                    W(eng, (g.pe_fw, ob))
                    bank = p_tr if (ob - 1) % 2 == 0 else p_pa_bf
                    eng.scalar_tensor_tensor(
                        fw3[:, :, (ob - 1) * 128: ob * 128],
                        bank[:, :].rearrange("p (i n) -> p i n", i=FB), 1.0,
                        s_fsc[:, (ob - 1) * 128: ob * 128].unsqueeze(1
                              ).broadcast_to([128, FB, 128]),
                        OP.mult, OP.mult).then_inc(g.dv_fwe, 1)
            W(eng, (g.pe_fw, FB))
            eng.scalar_tensor_tensor(
                fw3[:, :, (FB - 1) * 128: FB * 128],
                p_pa_bf[:, :].rearrange("p (i n) -> p i n", i=FB), 1.0,
                s_fsc[:, (FB - 1) * 128: FB * 128].unsqueeze(1
                      ).broadcast_to([128, FB, 128]),
                OP.mult, OP.mult).then_inc(g.dv_fwe, 1)
            W(eng, (g.pe_cx, NT))
            for hp in range(FB):
                off = ctx_off(hp)
                ins = eng.reciprocal(s_m[:, hp:hp + 1], p_c[:, off + 128:off + 129])
            chain(eng, ins, "dv")
            chain(eng, eng.tensor_scalar(s_m[:, :], s_m[:, :], S_SCALE, None,
                                         OP.mult), "dv")
            for hp in range(FB):
                off = ctx_off(hp)
                ins = eng.scalar_tensor_tensor(
                    s_ctx[:, hp * 128:(hp + 1) * 128], p_c[:, off:off + 128],
                    s_m[:, hp:hp + 1], s_sbv[:, hp * 128:(hp + 1) * 128],
                    OP.mult, OP.add)
            chain(eng, ins, "dv")
            for hp in range(FB):
                eng.memset(s_ctx[0:64, hp * 128 + 64: hp * 128 + 128], 0.0)
                ins = eng.memset(s_ctx[64:128, hp * 128: hp * 128 + 64], 0.0)
            ins.then_inc(g.dv_ctx, 1)
            for k in range(NT + 7):
                if 2 <= k < NT + 2:
                    c = k - 2
                    st = s_st[c % 2]
                    W(eng, (g.pe_o, k - 1), (g.gp_zb, k - 1),
                      (g.ac_y1, k - 3))
                    for hh in range(2):
                        ins = eng.scalar_tensor_tensor(
                            s_z1[c % 2][:, hh * 512:(hh + 1) * 512],
                            p_b[:, hh * 512:(hh + 1) * 512],
                            1.0, s_zb[c % 3][:, hh * 512:(hh + 1) * 512],
                            OP.mult, OP.add)
                    ins.then_inc(g.dv_z1, 1)
                    W(eng, (g.dv_z1, k - 1))
                    eng.bn_stats(st[:, 0:6], s_z1[c % 2][:, 0:512])
                    chain(eng, eng.bn_stats(st[:, 6:12], s_z1[c % 2][:, 512:1024]),
                          "dv")
                    eng.bn_aggr(st[:, 12:14], st[:, 0:12]).then_inc(g.dv_bn1, 1)
                    W(eng, (g.ac_r1, k - 1))
                    eng.tensor_scalar(st[:, 16:17], st[:, 12:13], st[:, 15:16],
                                      -1.0, OP.mult, OP.mult).then_inc(g.dv_nm1, 1)
                if 5 <= k < NT + 5:
                    c = k - 5
                    W(eng, (g.pe_f, k - 4))
                    for hh in range(2):
                        ins = eng.tensor_scalar(
                            s_yt[c % 2][:, hh * 512:(hh + 1) * 512],
                            p_c[:, hh * 512:(hh + 1) * 512],
                            100.0, -100.0, OP.min, OP.max)
                    ins.then_inc(g.dv_y2, 1)
                if 6 <= k < NT + 6:
                    c = k - 6
                    st = s_st[c % 2]
                    W(eng, (g.ac_y1, k - 5), (g.ac_out, k - 7))
                    chain(eng, eng.scalar_tensor_tensor(
                        s_yb[c % 3][:, :], s_yt[c % 2][:, :], 1.0,
                        s_y1b[c % 5][:, :], OP.mult, OP.add), "dv")
                    eng.bn_stats(st[:, 24:30], s_yb[c % 3][:, 0:512])
                    chain(eng, eng.bn_stats(st[:, 30:36], s_yb[c % 3][:, 512:1024]),
                          "dv")
                    eng.bn_aggr(st[:, 17:19], st[:, 24:36]).then_inc(g.dv_bn2, 1)
                if 7 <= k < NT + 7:
                    c = k - 7
                    st = s_st[c % 2]
                    W(eng, (g.ac_r2, k - 6))
                    eng.tensor_scalar(st[:, 21:22], st[:, 17:18], st[:, 20:21],
                                      -1.0, OP.mult, OP.mult).then_inc(g.dv_nm2, 1)

        def body_ac(eng):

            for c in range(2):
                W(eng, (dxb_s[c], 16))
                for f in range(5, 8):
                    eng.dma_start(s_xT[c % 2][:, f * 128:(f + 1) * 128],
                                  s_xb[c % 5][:, f * 128:(f + 1) * 128],
                                  transpose=True).then_inc(dtx_s[c % 2], 16)
            for k in range(NT + 1):
                if k < NT:
                    W(eng, (g.dv_q, k + 1), (g.dv_qs, k - 1))
                    eng.activation(s_ef[k % 2][:, :], s_qf[k % 2][:, :], AF.Exp).then_inc(g.ac_e, 1)
                if k >= 1:
                    c = k - 1
                    W(eng, (g.dv_qs, k), (g.pe_cx, k - 3))
                    eng.activation(s_ke[c % 3][:, :], s_qsb[c % 2][:, :], AF.Exp).then_inc(g.ac_ke, 1)
                if k + 2 < NT:
                    c = k + 2
                    W(eng, (dxb_s[c % 5], 16 * (c // 5 + 1)),
                      (g.pe_q, k + 1), (g.pe_v, k + 1))
                    for f in range(5, 8):
                        eng.dma_start(s_xT[c % 2][:, f * 128:(f + 1) * 128],
                                      s_xb[c % 5][:, f * 128:(f + 1) * 128],
                                      transpose=True).then_inc(dtx_s[c % 2], 16)
            for k in range(NT + 7):
                if k < NT:
                    W(eng, (g.pe_at, k + 1), (g.pe_o, k - 1))
                    eng.activation(s_aT[k % 2][:, 0:512], p_a[:, 0:512], AF.Copy)
                    eng.activation(s_aT[k % 2][:, 512:1024], p_a[:, 512:1024],
                                   AF.Copy).then_inc(g.ac_aT, 1)
                if 4 <= k < NT + 4:
                    c = k - 4
                    W(eng, (g.pe_y1t, k - 3), (g.pe_f, k - 5))
                    eng.activation(s_y1T[c % 2][:, :], p_tr[:, :], AF.Copy
                                   ).then_inc(g.dv_y1T, 1)
                if 2 <= k < NT + 2:
                    c = k - 2
                    st = s_st[c % 2]
                    W(eng, (g.dv_bn1, k - 1))
                    if k == 2:
                        W(eng, (g.dv_pre, 2))
                    chain(eng, eng.activation(st[:, 14:15], st[:, 13:14], AF.Ln,
                                              bias=s_cst[:, 0:1]), "ac")
                    eng.activation(st[:, 15:16], st[:, 14:15], AF.Exp,
                                   scale=s_cst[:, 1:2]).then_inc(g.ac_r1, 1)
                    W(eng, (g.dv_nm1, k - 1), (g.pe_y1t, k - 6), (g.dv_bn2, k - 6))
                    eng.activation(s_y1b[c % 5][:, :], s_z1[c % 2][:, :],
                                   AF.Identity, scale=st[:, 15:16],
                                   bias=st[:, 16:17]).then_inc(g.ac_y1, 1)
                if 7 <= k < NT + 7:
                    c = k - 7
                    st = s_st[c % 2]
                    W(eng, (g.dv_bn2, k - 6))
                    chain(eng, eng.activation(st[:, 19:20], st[:, 18:19], AF.Ln,
                                              bias=s_cst[:, 0:1]), "ac")
                    eng.activation(st[:, 20:21], st[:, 19:20], AF.Exp,
                                   scale=s_cst[:, 1:2]).then_inc(g.ac_r2, 1)
                    W(eng, (g.dv_nm2, k - 6))
                    if c >= 3:
                        W(eng, (dout_s[c % 3], 16 * (c // 3)))
                    eng.activation(s_of[c % 3][:, :], s_yb[c % 3][:, :],
                                   AF.Identity, scale=st[:, 20:21],
                                   bias=st[:, 21:22]).then_inc(g.ac_out, 1)
            eng.activation(s_cst[:, 2:3], s_cst[:, 0:1], AF.Identity
                           ).then_inc(g.ac_out, 1)

        with nc.Block("init") as b0:
            b0.sync(body_init)
        with nc.Block("main") as blk:
            blk.sync(body_sync)
            blk.gpsimd(body_gp)
            blk.tensor(body_pe)
            blk.vector(body_dv)
            blk.scalar(body_ac)

        es_b.close()
    return nc


_CACHE = {}


def kernel(**inputs):
    _ = inputs.get("L0_attn_Wk"), inputs.get("L0_attn_bk")
    if "nc" not in _CACHE:
        _CACHE["nc"] = build_nc()
    nc = _CACHE["nc"]
    x = np.ascontiguousarray(np.asarray(inputs["L0_attn_x"], dtype=np.float32))
    common = {
        "wq": np.ascontiguousarray(np.asarray(inputs["L0_attn_Wq"], np.float32)),
        "wv": np.ascontiguousarray(np.asarray(inputs["L0_attn_Wv"], np.float32)),
        "wo": np.ascontiguousarray(np.asarray(inputs["L0_attn_Wo"], np.float32)),
        "fw": np.ascontiguousarray(np.asarray(inputs["L0_ffn_weights"], np.float32)),
        "bq": np.ascontiguousarray(np.asarray(inputs["L0_attn_bq"], np.float32)),
        "bv": np.ascontiguousarray(np.asarray(inputs["L0_attn_bv"], np.float32)),
        "bo": np.ascontiguousarray(np.asarray(inputs["L0_attn_bo"], np.float32)),
        "fsc": np.ascontiguousarray(np.asarray(inputs["L0_ffn_scale"], np.float32)),
        "fbi": np.ascontiguousarray(np.asarray(inputs["L0_ffn_bias"], np.float32)),
        "ident": np.eye(128, dtype=ml_dtypes.bfloat16),
    }
    in_maps = [dict(common, x=np.ascontiguousarray(x[c])) for c in range(8)]
    res = run_bass_kernel_spmd(nc, in_maps, core_ids=list(range(8)))
    return np.stack([np.asarray(r["out"], np.float32) for r in res.results])


# version 24
# speedup vs baseline: 1.0067x; 1.0067x over previous
import contextlib
import numpy as np
import ml_dtypes
import concourse.bass as bass
import concourse.mybir as mybir
from concourse.alu_op_type import AluOpType as OP
from concourse.bass_utils import run_bass_kernel_spmd

F32 = mybir.dt.float32
BF16 = mybir.dt.bfloat16
AF = mybir.ActivationFunctionType
AX = mybir.AxisListType

N, D, H, DH = 4096, 1024, 16, 64
NT = N // 128
FB = D // 128
S_SCALE = float((DH ** 0.5) ** (-0.25))
EPS = 1e-5


def build_nc():
    nc = bass.Bass("TRN2", target_bir_lowering=False)
    x_h = nc.dram_tensor("x", [N, D], F32, kind="ExternalInput").ap()
    wq_h = nc.dram_tensor("wq", [D, D], F32, kind="ExternalInput").ap()
    wv_h = nc.dram_tensor("wv", [D, D], F32, kind="ExternalInput").ap()
    wo_h = nc.dram_tensor("wo", [D, D], F32, kind="ExternalInput").ap()
    fw_h = nc.dram_tensor("fw", [D, D], F32, kind="ExternalInput").ap()
    bq_h = nc.dram_tensor("bq", [D], F32, kind="ExternalInput").ap()
    bv_h = nc.dram_tensor("bv", [D], F32, kind="ExternalInput").ap()
    bo_h = nc.dram_tensor("bo", [D], F32, kind="ExternalInput").ap()
    fsc_h = nc.dram_tensor("fsc", [D], F32, kind="ExternalInput").ap()
    fbi_h = nc.dram_tensor("fbi", [D], F32, kind="ExternalInput").ap()
    id_h = nc.dram_tensor("ident", [128, 128], BF16, kind="ExternalInput").ap()
    out_h = nc.dram_tensor("out", [N, D], F32, kind="ExternalOutput").ap()

    es = contextlib.ExitStack()
    with es:
        sb = lambda name, shape, dt=F32: es.enter_context(nc.sbuf_tensor(name, shape, dt))
        s_wo = sb("s_wo", [128, FB * D], BF16)
        s_fw = sb("s_fw", [128, FB * D], BF16)
        s_qsT = sb("s_qsT", [128, FB * N], BF16)
        s_ctx = sb("s_ctx", [128, D], BF16)
        s_bq = sb("s_bq", [128, D])
        s_sbv = sb("s_sbv", [128, D])
        s_fsc = sb("s_fsc", [128, D])
        s_bo = sb("s_bo", [128, D], BF16)
        s_one = sb("s_one", [1, 128], BF16)
        s_fbi_row = sb("s_fbi_row", [1, D], BF16)
        s_id = sb("s_id", [128, 128], BF16)
        s_m = sb("s_m", [128, FB])
        s_cst = sb("s_cst", [128, 4])
        s_tgp = sb("s_tgp", [128, 1])
        p_tr = es.enter_context(nc.psum_tensor("p_tr", [128, 1024], BF16))
        p_a = es.enter_context(nc.psum_tensor("p_a", [128, 1024], F32))
        p_b = es.enter_context(nc.psum_tensor("p_b", [128, 1024], F32))
        p_c = es.enter_context(nc.psum_tensor("p_c", [128, 1536], F32))

        es_a = contextlib.ExitStack()
        sba = lambda name, shape, dt=F32: es_a.enter_context(nc.sbuf_tensor(name, shape, dt))
        s_wq = sba("s_wq", [128, FB * D], BF16)
        s_wv = sba("s_wv", [128, FB * D], BF16)
        s_xb = [sba(f"s_xb{i}", [128, D], BF16) for i in range(5)]
        s_xT = [sba(f"s_xT{i}", [128, D], BF16) for i in range(2)]
        s_qf = [sba(f"s_qf{i}", [128, D]) for i in range(2)]
        s_ef = [sba(f"s_ef{i}", [128, D]) for i in range(2)]
        s_ss = [sba(f"s_ss{i}", [128, H]) for i in range(2)]
        s_sr = [sba(f"s_sr{i}", [128, H]) for i in range(2)]
        s_qsb = [sba(f"s_qsb{i}", [128, D], BF16) for i in range(2)]
        s_ke = [sba(f"s_ke{i}", [128, D], BF16) for i in range(3)]
        s_vb = [sba(f"s_vb{i}", [128, FB * 129], BF16) for i in range(3)]
        s_fqb = [sba(f"s_fqb{i}", [128, D], BF16) for i in range(5)]
        s_fsg = [sba(f"s_fsg{i}", [128, D], BF16) for i in range(2)]
        es_a.close()

        es_b = contextlib.ExitStack()
        sbb = lambda name, shape, dt=F32: es_b.enter_context(nc.sbuf_tensor(name, shape, dt))
        s_xr = [sbb(f"s_xr{i}", [128, D]) for i in range(5)]
        s_zb = [sbb(f"s_zb{i}", [128, D]) for i in range(3)]
        s_aT = [sbb(f"s_aT{i}", [128, D], BF16) for i in range(2)]
        s_z1 = [sbb(f"s_z1{i}", [128, D]) for i in range(2)]
        s_y1b = [sbb(f"s_y1b{i}", [128, D], BF16) for i in range(5)]
        s_y1T = [sbb(f"s_y1T{i}", [128, D], BF16) for i in range(2)]
        s_yt = [sbb(f"s_yt{i}", [128, D]) for i in range(2)]
        s_yb = [sbb(f"s_yb{i}", [128, D]) for i in range(3)]
        s_of = [sbb(f"s_of{i}", [128, D]) for i in range(3)]
        s_st = [sbb(f"s_st{i}", [128, 40]) for i in range(2)]

        sems = {}
        for i in range(5):
            sems[f"dxb{i}"] = es.enter_context(nc.semaphore(f"dxb{i}"))
        for i in range(5):
            sems[f"dxr{i}"] = es.enter_context(nc.semaphore(f"dxr{i}"))
        for i in range(FB):
            sems[f"dwq{i}"] = es.enter_context(nc.semaphore(f"dwq{i}"))
            sems[f"dwv{i}"] = es.enter_context(nc.semaphore(f"dwv{i}"))
        for i in range(5):
            sems[f"dfw{i}"] = es.enter_context(nc.semaphore(f"dfw{i}"))
        for i in range(3):
            sems[f"dout{i}"] = es.enter_context(nc.semaphore(f"dout{i}"))
        for name in ["dwo", "did", "dbo", "dbq",
                     "dbi", "dwt", "dfw", "dxb", "dxr", "dout",
                     "dv_pre", "dv_fgs", "pe_fw", "dv_fwe",
                     "pe_xt", "dv_xt", "pe_q", "dv_q", "ac_e", "dv_qs",
                     "ac_ke", "pe_qt", "dv_qt", "pe_v", "dv_v", "pe_cx",
                     "dv_ctx",
                     "pe_at", "ac_aT", "pe_o", "dv_z1", "dv_bn1", "ac_r1",
                     "dv_nm1", "ac_y1", "pe_y1t", "dv_y1T", "pe_f", "dv_y2",
                     "gp_zb", "gp_yc", "dv_bn2", "ac_r2", "dv_nm2", "ac_out", "drow",
                     "dv_sm", "ac_sm"]:
            sems[name] = es.enter_context(nc.semaphore(name))
        g = type("G", (), sems)
        dxb_s = [sems[f"dxb{i}"] for i in range(5)]
        dxr_s = [sems[f"dxr{i}"] for i in range(5)]
        dwq_s = [sems[f"dwq{i}"] for i in range(FB)]
        dwv_s = [sems[f"dwv{i}"] for i in range(FB)]
        dfw_s = [sems[f"dfw{i}"] for i in range(5)]
        dout_s = [sems[f"dout{i}"] for i in range(3)]

        def W(eng, *pairs):
            for sem, v in pairs:
                if v >= 1:
                    eng.wait_ge(sem, v)

        _smc = {"dv": 0, "ac": 0}
        def chain(eng, ins, which):
            sem = g.dv_sm if which == "dv" else g.ac_sm
            _smc[which] += 1
            ins.then_inc(sem, 1)
            eng.wait_ge(sem, _smc[which])

        qsT3 = s_qsT[:, :].rearrange("p (f n) -> p f n", f=FB)
        p_pa_bf = p_a[:, :].bitcast(BF16)[:, 0:1024]
        p_pc_bf = p_c[:, :].bitcast(BF16)[:, 2048:3072]
        fw3 = s_fw[:, :].rearrange("p (ib n) -> p ib n", ib=FB)
        ptr3 = p_tr[:, :].rearrange("p (f n) -> p f n", f=FB)
        vb3 = [s_vb[i][:, :].rearrange("p (h c) -> p h c", h=FB) for i in range(3)]
        pb3 = p_b[:, :].rearrange("p (h c) -> p h c", h=FB)

        def ctx_off(hp):
            return (hp // 3) * 512 + (hp % 3) * 129

        def body_init(eng):
            for s in sems.values():
                eng.sem_clear(s)

        def body_sync(eng):
            for j in range(NT):
                W(eng, (g.ac_out, j + 2))
                eng.dma_start(out_h[j * 128:(j + 1) * 128, :], s_of[j % 3][:, :]
                              ).then_inc(dout_s[j % 3], 16)
            for s in range(3):
                eng.wait_ge(dout_s[s], 16 * len([j for j in range(NT) if j % 3 == s]))

        def body_gp(eng):
            eng.dma_start(s_id[:, :], id_h).then_inc(g.did, 16)
            for j in range(2):
                eng.dma_start(s_xb[j][:, :], x_h[j * 128:(j + 1) * 128, :]
                              ).then_inc(dxb_s[j], 16)
            eng.dma_start(s_bq[:, :], bq_h.unsqueeze(0).broadcast_to([128, D])
                          ).then_inc(g.dbq, 16)
            for dst, src in [(s_sbv, bv_h), (s_fsc, fsc_h)]:
                eng.dma_start(dst[:, :], src.unsqueeze(0).broadcast_to([128, D])
                              ).then_inc(g.dbi, 16)
            eng.dma_start(s_fbi_row[:, :], fbi_h.unsqueeze(0)).then_inc(g.drow, 16)
            eng.dma_start(s_bo[:, :], bo_h.unsqueeze(0).broadcast_to([128, D])
                          ).then_inc(g.dbo, 16)
            for kb in range(FB):
                eng.dma_start(s_wq[:, kb * D:(kb + 1) * D],
                              wq_h[kb * 128:(kb + 1) * 128, :]
                              ).then_inc(dwq_s[kb], 16)
            for j in range(2, 4):
                eng.dma_start(s_xb[j][:, :], x_h[j * 128:(j + 1) * 128, :]
                              ).then_inc(dxb_s[j], 16)
            for kb in range(FB):
                eng.dma_start(s_wv[:, kb * D:(kb + 1) * D],
                              wv_h[kb * 128:(kb + 1) * 128, :]
                              ).then_inc(dwv_s[kb], 16)
            eng.dma_start(s_wo[:, :].rearrange("p (kb n) -> p kb n", kb=FB),
                          wo_h.rearrange("(kb p) n -> p kb n", p=128)
                          ).then_inc(g.dwo, 16)
            eng.dma_start(s_xb[4][:, :], x_h[4 * 128:5 * 128, :]
                          ).then_inc(dxb_s[4], 16)
            for j in range(5, NT):
                W(eng, (g.pe_xt, j - 4))
                eng.dma_start(s_xb[j % 5][:, :], x_h[j * 128:(j + 1) * 128, :]
                              ).then_inc(dxb_s[j % 5], 16)
                if j == 8:
                    for ob in range(3):
                        eng.dma_start(s_fqb[ob][:, :],
                                      fw_h[ob * 128:(ob + 1) * 128, :]
                                      ).then_inc(dfw_s[ob], 16)
                if j == 10:
                    for ob in range(3, 5):
                        eng.dma_start(s_fqb[ob][:, :],
                                      fw_h[ob * 128:(ob + 1) * 128, :]
                                      ).then_inc(dfw_s[ob], 16)
            for ob in range(5, FB):
                W(eng, (g.dv_fgs, ob - 4))
                eng.dma_start(s_fqb[ob % 5][:, :],
                              fw_h[ob * 128:(ob + 1) * 128, :]
                              ).then_inc(dfw_s[ob % 5], 16)
            W(eng, (g.dv_ctx, 1))
            for j in range(4):
                eng.dma_start(s_xr[j][:, :], x_h[j * 128:(j + 1) * 128, :]
                              ).then_inc(dxr_s[j], 16)
            for k in range(NT):
                W(eng, (dxr_s[k % 5], 16 * (k // 5 + 1)), (g.dv_z1, k - 2))
                if k == 0:
                    W(eng, (g.dbo, 16))
                eng.tensor_tensor(s_zb[k % 3][:, :], s_xr[k % 5][:, :],
                                  s_bo[:, :], OP.add).then_inc(g.gp_zb, 1)
                if k + 4 < NT:
                    W(eng, (g.gp_zb, k))
                    eng.dma_start(s_xr[(k + 4) % 5][:, :],
                                  x_h[(k + 4) * 128:(k + 5) * 128, :]
                                  ).then_inc(dxr_s[(k + 4) % 5], 16)

        def body_pe(eng):
            W(eng, (g.did, 16), (dxb_s[0], 16))
            for f in range(FB):
                ins = eng.transpose(p_tr[:, f * 128:(f + 1) * 128],
                                    s_xb[0][:, f * 128:(f + 1) * 128], s_id[:, :])
            ins.then_inc(g.pe_xt, 1)
            for k in range(NT + 2):
                if k + 1 < NT:
                    c = k + 1
                    W(eng, (dxb_s[c % 5], 16 * (c // 5 + 1)), (g.dv_qt, k - 2),
                      (g.dv_xt, c))
                    for f in range(FB):
                        ins = eng.transpose(p_tr[:, f * 128:(f + 1) * 128],
                                            s_xb[c % 5][:, f * 128:(f + 1) * 128],
                                            s_id[:, :])
                    ins.then_inc(g.pe_xt, 1)
                if k >= 2:
                    c = k - 2
                    W(eng, (g.ac_ke, k - 1), (g.dv_v, k - 1))
                    if c == 0:
                        W(eng, (g.dv_pre, 2))
                    for hp in range(FB):
                        off = ctx_off(hp)
                        ins = eng.matmul(p_c[:, off:off + 129],
                                         s_ke[c % 3][:, hp * 128:(hp + 1) * 128],
                                         s_vb[c % 3][:, hp * 129:(hp + 1) * 129],
                                         start=(c == 0 and hp % 3 == 0),
                                         stop=(c == NT - 1 and hp in (2, 5, 7)),
                                         skip_group_check=True)
                    ins.then_inc(g.pe_cx, 1)
                if k < NT:
                    W(eng, (g.dv_xt, k + 1), (g.dv_q, k))
                    for kb in range(FB):
                        if k == 0:
                            W(eng, (dwq_s[kb], 16))
                        for hh in range(2):
                            ins = eng.matmul(
                                p_a[:, hh * 512:(hh + 1) * 512],
                                s_xT[k % 2][:, kb * 128:(kb + 1) * 128],
                                s_wq[:, kb * D + hh * 512: kb * D + (hh + 1) * 512],
                                start=(kb == 0), stop=(kb == FB - 1))
                    ins.then_inc(g.pe_q, 1)
                if k >= 2:
                    c = k - 2
                    W(eng, (g.dv_qs, k - 1), (g.dv_xt, min(k + 2, NT)),
                      (g.dv_qt, k - 2))
                    for f in range(FB):
                        ins = eng.transpose(p_tr[:, f * 128:(f + 1) * 128],
                                            s_qsb[c % 2][:, f * 128:(f + 1) * 128],
                                            s_id[:, :])
                    ins.then_inc(g.pe_qt, 1)
                if k < NT:
                    W(eng, (g.dv_v, k))
                    for kb in range(FB):
                        if k == 0:
                            W(eng, (dwv_s[kb], 16))
                        for hh in range(2):
                            ins = eng.matmul(
                                p_b[:, hh * 512:(hh + 1) * 512],
                                s_xT[k % 2][:, kb * 128:(kb + 1) * 128],
                                s_wv[:, kb * D + hh * 512: kb * D + (hh + 1) * 512],
                                start=(kb == 0), stop=(kb == FB - 1))
                    ins.then_inc(g.pe_v, 1)
            for k in range(NT + 7):
                if k < NT:
                    W(eng, (g.ac_aT, k))
                    if k == 0:
                        W(eng, (g.dv_ctx, 1))
                    for hp in range(FB):
                        ins = eng.matmul(
                            p_a[:, hp * 128:(hp + 1) * 128],
                            s_ctx[:, hp * 128:(hp + 1) * 128],
                            s_qsT[:, hp * N + k * 128: hp * N + (k + 1) * 128],
                            start=True, stop=True, skip_group_check=True)
                    ins.then_inc(g.pe_at, 1)
                if 1 <= k <= 2:
                    for ob in range(4 * (k - 1), 4 * k):
                        W(eng, (g.dv_fgs, ob + 1), (g.dv_fwe, ob - 1))
                        if ob == 0:
                            W(eng, (g.dv_qt, NT))
                        bank = p_tr if ob % 2 == 0 else p_pc_bf
                        for ib in range(FB):
                            ins = eng.transpose(bank[:, ib * 128:(ib + 1) * 128],
                                                s_fsg[ob % 2][:, ib * 128:(ib + 1) * 128],
                                                s_id[:, :])
                        ins.then_inc(g.pe_fw, 1)
                if 4 <= k < NT + 4:
                    c = k - 4
                    W(eng, (g.ac_y1, k - 3), (g.dv_y1T, k - 4))
                    if k == 4:
                        W(eng, (g.dv_fwe, FB))
                    for f in range(FB):
                        ins = eng.transpose(p_tr[:, f * 128:(f + 1) * 128],
                                            s_y1b[c % 5][:, f * 128:(f + 1) * 128],
                                            s_id[:, :])
                    ins.then_inc(g.pe_y1t, 1)
                if 5 <= k < NT + 5:
                    c = k - 5
                    W(eng, (g.dv_y1T, k - 4), (g.dv_y2, k - 5))
                    if k == 5:
                        W(eng, (g.dv_ctx, 1), (g.dv_fwe, FB), (g.drow, 16),
                          (g.dv_pre, 2))
                    for kb in range(FB):
                        for hh in range(2):
                            ins = eng.matmul(
                                p_c[:, hh * 512:(hh + 1) * 512],
                                s_y1T[c % 2][:, kb * 128:(kb + 1) * 128],
                                s_fw[:, kb * D + hh * 512: kb * D + (hh + 1) * 512],
                                start=(kb == 0), stop=False)
                    for hh in range(2):
                        ins = eng.matmul(p_c[:, hh * 512:(hh + 1) * 512],
                                         s_one[0:1, 0:128],
                                         s_fbi_row[0:1, hh * 512:(hh + 1) * 512],
                                         start=False, stop=True)
                    ins.then_inc(g.pe_f, 1)
                if 1 <= k < NT + 1:
                    c = k - 1
                    W(eng, (g.ac_aT, k), (g.dv_z1, k - 1))
                    if k == 1:
                        W(eng, (g.dwo, 16), (g.dv_v, NT),
                          (g.drow, 16), (g.dv_pre, 2))
                    for kb in range(FB):
                        for hh in range(2):
                            ins = eng.matmul(
                                p_b[:, hh * 512:(hh + 1) * 512],
                                s_aT[c % 2][:, kb * 128:(kb + 1) * 128],
                                s_wo[:, kb * D + hh * 512: kb * D + (hh + 1) * 512],
                                start=(kb == 0), stop=(kb == FB - 1))
                    ins.then_inc(g.pe_o, 1)

        def body_dv(eng):
            eng.memset(s_cst[:, 0:1], EPS)
            eng.memset(s_cst[:, 1:2], -0.5)
            eng.memset(vb3[0][:, :, 128:129], 1.0)
            eng.memset(vb3[1][:, :, 128:129], 1.0)
            eng.memset(vb3[2][:, :, 128:129], 1.0)
            eng.memset(s_one[:, :], 1.0)
            for hp in range(FB):
                eng.memset(s_ctx[0:64, hp * 128 + 64: hp * 128 + 128], 0.0)
                eng.memset(s_ctx[64:128, hp * 128: hp * 128 + 64], 0.0)
            W(eng, (g.dbi, 32))
            eng.tensor_scalar(s_sbv[:, :], s_sbv[:, :], S_SCALE, None, OP.mult
                              ).then_inc(g.dv_pre, 1)
            eng.tensor_scalar(s_fsc[:, :], s_fsc[:, :], 2.0, None, OP.mult
                              ).then_inc(g.dv_pre, 1)

            W(eng, (g.pe_xt, 1))
            eng.tensor_copy(s_xT[0][:, :], p_tr[:, :]).then_inc(g.dv_xt, 1)
            for k in range(NT + 2):
                if k + 1 < NT:
                    c = k + 1
                    W(eng, (g.pe_xt, c + 1), (g.pe_v, k))
                    eng.tensor_copy(s_xT[c % 2][:, :], p_tr[:, :]).then_inc(g.dv_xt, 1)
                if 1 <= k <= NT:
                    c = k - 1
                    W(eng, (g.ac_e, k))
                    chain(eng, eng.tensor_reduce(
                        s_ss[c % 2][:, :],
                        s_ef[c % 2][:, :].rearrange("p (h w) -> p h w", h=H),
                        AX.X, OP.add), "dv")
                    chain(eng, eng.reciprocal(s_sr[c % 2][:, :], s_ss[c % 2][:, :]),
                          "dv")
                    W(eng, (g.ac_ke, k - 2), (g.pe_qt, k - 2))
                    eng.scalar_tensor_tensor(
                        s_qsb[c % 2][:, :].rearrange("p (h w) -> p h w", h=H),
                        s_ef[c % 2][:, :].rearrange("p (h w) -> p h w", h=H),
                        S_SCALE,
                        s_sr[c % 2][:, :].unsqueeze(2).broadcast_to([128, H, DH]),
                        OP.mult, OP.mult).then_inc(g.dv_qs, 1)
                if k < NT:
                    W(eng, (g.pe_q, k + 1), (g.ac_e, k - 1))
                    if k == 0:
                        W(eng, (g.dbq, 16))
                    for hh in range(2):
                        ins = eng.scalar_tensor_tensor(
                            s_qf[k % 2][:, hh * 512:(hh + 1) * 512],
                            p_a[:, hh * 512:(hh + 1) * 512],
                            1.0, s_bq[:, hh * 512:(hh + 1) * 512],
                            OP.mult, OP.add)
                    ins.then_inc(g.dv_q, 1)
                if k >= 2:
                    c = k - 2
                    W(eng, (g.pe_qt, k - 1))
                    eng.tensor_copy(qsT3[:, :, c * 128:(c + 1) * 128],
                                    ptr3[:, :, :]).then_inc(g.dv_qt, 1)
                if k < NT:
                    W(eng, (g.pe_v, k + 1), (g.pe_cx, k - 2))
                    eng.tensor_copy(vb3[k % 3][:, :, 0:128], pb3[:, :, :]).then_inc(g.dv_v, 1)
            W(eng, (g.pe_cx, NT))
            for hp in range(FB):
                off = ctx_off(hp)
                ins = eng.reciprocal(s_m[:, hp:hp + 1], p_c[:, off + 128:off + 129])
            chain(eng, ins, "dv")
            chain(eng, eng.tensor_scalar(s_m[:, :], s_m[:, :], S_SCALE, None,
                                         OP.mult), "dv")
            for hp in range(FB):
                off = ctx_off(hp)
                eng.scalar_tensor_tensor(
                    s_ctx[0:64, hp * 128: hp * 128 + 64],
                    p_c[0:64, off:off + 64],
                    s_m[0:64, hp:hp + 1],
                    s_sbv[0:64, hp * 128: hp * 128 + 64],
                    OP.mult, OP.add)
                ins = eng.scalar_tensor_tensor(
                    s_ctx[64:128, hp * 128 + 64: (hp + 1) * 128],
                    p_c[64:128, off + 64:off + 128],
                    s_m[64:128, hp:hp + 1],
                    s_sbv[64:128, hp * 128 + 64: (hp + 1) * 128],
                    OP.mult, OP.add)
            ins.then_inc(g.dv_ctx, 1)
            for ob in range(FB):
                W(eng, (dfw_s[ob % 5], 16 * (ob // 5 + 1)))
                W(eng, (g.pe_fw, ob - 1))
                eng.tensor_scalar(s_fsg[ob % 2][:, :], s_fqb[ob % 5][:, :],
                                  0.0, -0.5, OP.is_ge, OP.add
                                  ).then_inc(g.dv_fgs, 1)
                if ob >= 1:
                    W(eng, (g.pe_fw, ob))
                    bank = p_tr if (ob - 1) % 2 == 0 else p_pc_bf
                    eng.scalar_tensor_tensor(
                        fw3[:, :, (ob - 1) * 128: ob * 128],
                        bank[:, :].rearrange("p (i n) -> p i n", i=FB), 1.0,
                        s_fsc[:, (ob - 1) * 128: ob * 128].unsqueeze(1
                              ).broadcast_to([128, FB, 128]),
                        OP.mult, OP.mult).then_inc(g.dv_fwe, 1)
            W(eng, (g.pe_fw, FB))
            eng.scalar_tensor_tensor(
                fw3[:, :, (FB - 1) * 128: FB * 128],
                p_pc_bf[:, :].rearrange("p (i n) -> p i n", i=FB), 1.0,
                s_fsc[:, (FB - 1) * 128: FB * 128].unsqueeze(1
                      ).broadcast_to([128, FB, 128]),
                OP.mult, OP.mult).then_inc(g.dv_fwe, 1)
            for k in range(NT + 7):
                if 2 <= k < NT + 2:
                    c = k - 2
                    st = s_st[c % 2]
                    W(eng, (g.pe_o, k - 1), (g.gp_zb, k - 1),
                      (g.ac_y1, k - 3))
                    for hh in range(2):
                        ins = eng.scalar_tensor_tensor(
                            s_z1[c % 2][:, hh * 512:(hh + 1) * 512],
                            p_b[:, hh * 512:(hh + 1) * 512],
                            1.0, s_zb[c % 3][:, hh * 512:(hh + 1) * 512],
                            OP.mult, OP.add)
                    ins.then_inc(g.dv_z1, 1)
                    W(eng, (g.dv_z1, k - 1))
                    eng.bn_stats(st[:, 0:6], s_z1[c % 2][:, 0:512])
                    chain(eng, eng.bn_stats(st[:, 6:12], s_z1[c % 2][:, 512:1024]),
                          "dv")
                    eng.bn_aggr(st[:, 12:14], st[:, 0:12]).then_inc(g.dv_bn1, 1)
                    W(eng, (g.ac_r1, k - 1))
                    eng.tensor_scalar(st[:, 16:17], st[:, 12:13], st[:, 15:16],
                                      -1.0, OP.mult, OP.mult).then_inc(g.dv_nm1, 1)
                if 5 <= k < NT + 5:
                    c = k - 5
                    W(eng, (g.pe_f, k - 4))
                    for hh in range(2):
                        ins = eng.tensor_scalar(
                            s_yt[c % 2][:, hh * 512:(hh + 1) * 512],
                            p_c[:, hh * 512:(hh + 1) * 512],
                            100.0, -100.0, OP.min, OP.max)
                    ins.then_inc(g.dv_y2, 1)
                if 6 <= k < NT + 6:
                    c = k - 6
                    st = s_st[c % 2]
                    W(eng, (g.ac_y1, k - 5), (g.ac_out, k - 7))
                    chain(eng, eng.scalar_tensor_tensor(
                        s_yb[c % 3][:, :], s_yt[c % 2][:, :], 1.0,
                        s_y1b[c % 5][:, :], OP.mult, OP.add), "dv")
                    eng.bn_stats(st[:, 24:30], s_yb[c % 3][:, 0:512])
                    chain(eng, eng.bn_stats(st[:, 30:36], s_yb[c % 3][:, 512:1024]),
                          "dv")
                    eng.bn_aggr(st[:, 17:19], st[:, 24:36]).then_inc(g.dv_bn2, 1)
                if 7 <= k < NT + 7:
                    c = k - 7
                    st = s_st[c % 2]
                    W(eng, (g.ac_r2, k - 6))
                    eng.tensor_scalar(st[:, 21:22], st[:, 17:18], st[:, 20:21],
                                      -1.0, OP.mult, OP.mult).then_inc(g.dv_nm2, 1)

        def body_ac(eng):

            for k in range(NT + 1):
                if k < NT:
                    W(eng, (g.dv_q, k + 1), (g.dv_qs, k - 1))
                    eng.activation(s_ef[k % 2][:, :], s_qf[k % 2][:, :], AF.Exp).then_inc(g.ac_e, 1)
                if k >= 1:
                    c = k - 1
                    W(eng, (g.dv_qs, k), (g.pe_cx, k - 3))
                    eng.activation(s_ke[c % 3][:, :], s_qsb[c % 2][:, :], AF.Exp).then_inc(g.ac_ke, 1)
            for k in range(NT + 7):
                if k < NT:
                    W(eng, (g.pe_at, k + 1), (g.pe_o, k - 1))
                    eng.activation(s_aT[k % 2][:, 0:512], p_a[:, 0:512], AF.Copy)
                    eng.activation(s_aT[k % 2][:, 512:1024], p_a[:, 512:1024],
                                   AF.Copy).then_inc(g.ac_aT, 1)
                if 4 <= k < NT + 4:
                    c = k - 4
                    W(eng, (g.pe_y1t, k - 3), (g.pe_f, k - 5))
                    eng.activation(s_y1T[c % 2][:, :], p_tr[:, :], AF.Copy
                                   ).then_inc(g.dv_y1T, 1)
                if 2 <= k < NT + 2:
                    c = k - 2
                    st = s_st[c % 2]
                    W(eng, (g.dv_bn1, k - 1))
                    if k == 2:
                        W(eng, (g.dv_pre, 2))
                    chain(eng, eng.activation(st[:, 14:15], st[:, 13:14], AF.Ln,
                                              bias=s_cst[:, 0:1]), "ac")
                    eng.activation(st[:, 15:16], st[:, 14:15], AF.Exp,
                                   scale=s_cst[:, 1:2]).then_inc(g.ac_r1, 1)
                    W(eng, (g.dv_nm1, k - 1), (g.pe_y1t, k - 6), (g.dv_bn2, k - 6))
                    eng.activation(s_y1b[c % 5][:, :], s_z1[c % 2][:, :],
                                   AF.Identity, scale=st[:, 15:16],
                                   bias=st[:, 16:17]).then_inc(g.ac_y1, 1)
                if 7 <= k < NT + 7:
                    c = k - 7
                    st = s_st[c % 2]
                    W(eng, (g.dv_bn2, k - 6))
                    chain(eng, eng.activation(st[:, 19:20], st[:, 18:19], AF.Ln,
                                              bias=s_cst[:, 0:1]), "ac")
                    eng.activation(st[:, 20:21], st[:, 19:20], AF.Exp,
                                   scale=s_cst[:, 1:2]).then_inc(g.ac_r2, 1)
                    W(eng, (g.dv_nm2, k - 6))
                    if c >= 3:
                        W(eng, (dout_s[c % 3], 16 * (c // 3)))
                    eng.activation(s_of[c % 3][:, :], s_yb[c % 3][:, :],
                                   AF.Identity, scale=st[:, 20:21],
                                   bias=st[:, 21:22]).then_inc(g.ac_out, 1)
            eng.activation(s_cst[:, 2:3], s_cst[:, 0:1], AF.Identity
                           ).then_inc(g.ac_out, 1)

        with nc.Block("init") as b0:
            b0.sync(body_init)
        with nc.Block("main") as blk:
            blk.sync(body_sync)
            blk.gpsimd(body_gp)
            blk.tensor(body_pe)
            blk.vector(body_dv)
            blk.scalar(body_ac)

        es_b.close()
    return nc


_CACHE = {}


def kernel(**inputs):
    _ = inputs.get("L0_attn_Wk"), inputs.get("L0_attn_bk")
    if "nc" not in _CACHE:
        _CACHE["nc"] = build_nc()
    nc = _CACHE["nc"]
    x = np.ascontiguousarray(np.asarray(inputs["L0_attn_x"], dtype=np.float32))
    common = {
        "wq": np.ascontiguousarray(np.asarray(inputs["L0_attn_Wq"], np.float32)),
        "wv": np.ascontiguousarray(np.asarray(inputs["L0_attn_Wv"], np.float32)),
        "wo": np.ascontiguousarray(np.asarray(inputs["L0_attn_Wo"], np.float32)),
        "fw": np.ascontiguousarray(np.asarray(inputs["L0_ffn_weights"], np.float32)),
        "bq": np.ascontiguousarray(np.asarray(inputs["L0_attn_bq"], np.float32)),
        "bv": np.ascontiguousarray(np.asarray(inputs["L0_attn_bv"], np.float32)),
        "bo": np.ascontiguousarray(np.asarray(inputs["L0_attn_bo"], np.float32)),
        "fsc": np.ascontiguousarray(np.asarray(inputs["L0_ffn_scale"], np.float32)),
        "fbi": np.ascontiguousarray(np.asarray(inputs["L0_ffn_bias"], np.float32)),
        "ident": np.eye(128, dtype=ml_dtypes.bfloat16),
    }
    in_maps = [dict(common, x=np.ascontiguousarray(x[c])) for c in range(8)]
    res = run_bass_kernel_spmd(nc, in_maps, core_ids=list(range(8)))
    return np.stack([np.asarray(r["out"], np.float32) for r in res.results])


# version 25
# speedup vs baseline: 1.0076x; 1.0009x over previous
import contextlib
import numpy as np
import ml_dtypes
import concourse.bass as bass
import concourse.mybir as mybir
from concourse.alu_op_type import AluOpType as OP
from concourse.bass_utils import run_bass_kernel_spmd

F32 = mybir.dt.float32
BF16 = mybir.dt.bfloat16
AF = mybir.ActivationFunctionType
AX = mybir.AxisListType

N, D, H, DH = 4096, 1024, 16, 64
NT = N // 128
FB = D // 128
S_SCALE = float((DH ** 0.5) ** (-0.25))
EPS = 1e-5


def build_nc():
    nc = bass.Bass("TRN2", target_bir_lowering=False)
    x_h = nc.dram_tensor("x", [N, D], F32, kind="ExternalInput").ap()
    wq_h = nc.dram_tensor("wq", [D, D], F32, kind="ExternalInput").ap()
    wv_h = nc.dram_tensor("wv", [D, D], F32, kind="ExternalInput").ap()
    wo_h = nc.dram_tensor("wo", [D, D], F32, kind="ExternalInput").ap()
    fw_h = nc.dram_tensor("fw", [D, D], F32, kind="ExternalInput").ap()
    bq_h = nc.dram_tensor("bq", [D], F32, kind="ExternalInput").ap()
    bv_h = nc.dram_tensor("bv", [D], F32, kind="ExternalInput").ap()
    bo_h = nc.dram_tensor("bo", [D], F32, kind="ExternalInput").ap()
    fsc_h = nc.dram_tensor("fsc", [D], F32, kind="ExternalInput").ap()
    fbi_h = nc.dram_tensor("fbi", [D], F32, kind="ExternalInput").ap()
    id_h = nc.dram_tensor("ident", [128, 128], BF16, kind="ExternalInput").ap()
    out_h = nc.dram_tensor("out", [N, D], F32, kind="ExternalOutput").ap()

    es = contextlib.ExitStack()
    with es:
        sb = lambda name, shape, dt=F32: es.enter_context(nc.sbuf_tensor(name, shape, dt))
        s_wo = sb("s_wo", [128, FB * D], BF16)
        s_fw = sb("s_fw", [128, FB * D], BF16)
        s_qsT = sb("s_qsT", [128, FB * N], BF16)
        s_ctx = sb("s_ctx", [128, D], BF16)
        s_bq = sb("s_bq", [128, D])
        s_sbv = sb("s_sbv", [128, D])
        s_fsc = sb("s_fsc", [128, D])
        s_bo = sb("s_bo", [128, D], BF16)
        s_one = sb("s_one", [1, 128], BF16)
        s_fbi_row = sb("s_fbi_row", [1, D], BF16)
        s_id = sb("s_id", [128, 128], BF16)
        s_m = sb("s_m", [128, FB])
        s_cst = sb("s_cst", [128, 4])
        s_tgp = sb("s_tgp", [128, 1])
        p_tr = es.enter_context(nc.psum_tensor("p_tr", [128, 1024], BF16))
        p_a = es.enter_context(nc.psum_tensor("p_a", [128, 1024], F32))
        p_b = es.enter_context(nc.psum_tensor("p_b", [128, 1024], F32))
        p_c = es.enter_context(nc.psum_tensor("p_c", [128, 1536], F32))

        es_a = contextlib.ExitStack()
        sba = lambda name, shape, dt=F32: es_a.enter_context(nc.sbuf_tensor(name, shape, dt))
        s_wq = sba("s_wq", [128, FB * D], BF16)
        s_wv = sba("s_wv", [128, FB * D], BF16)
        s_xb = [sba(f"s_xb{i}", [128, D], BF16) for i in range(5)]
        s_xT = [sba(f"s_xT{i}", [128, D], BF16) for i in range(2)]
        s_qf = [sba(f"s_qf{i}", [128, D]) for i in range(2)]
        s_ef = [sba(f"s_ef{i}", [128, D]) for i in range(2)]
        s_ss = [sba(f"s_ss{i}", [128, H]) for i in range(2)]
        s_sr = [sba(f"s_sr{i}", [128, H]) for i in range(2)]
        s_qsb = [sba(f"s_qsb{i}", [128, D], BF16) for i in range(2)]
        s_ke = [sba(f"s_ke{i}", [128, D], BF16) for i in range(3)]
        s_vb = [sba(f"s_vb{i}", [128, FB * 129], BF16) for i in range(3)]
        s_fqb = [sba(f"s_fqb{i}", [128, D], BF16) for i in range(5)]
        s_fsg = [sba(f"s_fsg{i}", [128, D], BF16) for i in range(2)]
        es_a.close()

        es_b = contextlib.ExitStack()
        sbb = lambda name, shape, dt=F32: es_b.enter_context(nc.sbuf_tensor(name, shape, dt))
        s_xr = [sbb(f"s_xr{i}", [128, D]) for i in range(5)]
        s_zb = [sbb(f"s_zb{i}", [128, D]) for i in range(3)]
        s_aT = [sbb(f"s_aT{i}", [128, D], BF16) for i in range(2)]
        s_z1 = [sbb(f"s_z1{i}", [128, D]) for i in range(2)]
        s_y1b = [sbb(f"s_y1b{i}", [128, D], BF16) for i in range(5)]
        s_y1T = [sbb(f"s_y1T{i}", [128, D], BF16) for i in range(2)]
        s_yt = [sbb(f"s_yt{i}", [128, D]) for i in range(2)]
        s_yb = [sbb(f"s_yb{i}", [128, D]) for i in range(3)]
        s_of = [sbb(f"s_of{i}", [128, D]) for i in range(3)]
        s_st = [sbb(f"s_st{i}", [128, 40]) for i in range(2)]

        sems = {}
        for i in range(5):
            sems[f"dxb{i}"] = es.enter_context(nc.semaphore(f"dxb{i}"))
        for i in range(5):
            sems[f"dxr{i}"] = es.enter_context(nc.semaphore(f"dxr{i}"))
        for i in range(FB):
            sems[f"dwq{i}"] = es.enter_context(nc.semaphore(f"dwq{i}"))
            sems[f"dwv{i}"] = es.enter_context(nc.semaphore(f"dwv{i}"))
        for i in range(5):
            sems[f"dfw{i}"] = es.enter_context(nc.semaphore(f"dfw{i}"))
        for i in range(3):
            sems[f"dout{i}"] = es.enter_context(nc.semaphore(f"dout{i}"))
        for name in ["dwo", "did", "dbo",
                     "dbi", "dwt", "dfw", "dxb", "dxr", "dout",
                     "dv_pre", "dv_fgs", "pe_fw", "dv_fwe",
                     "pe_xt", "dv_xt", "pe_q", "dv_q", "ac_e", "dv_qs",
                     "ac_ke", "pe_qt", "dv_qt", "pe_v", "dv_v", "pe_cx",
                     "dv_ctx",
                     "pe_at", "ac_aT", "pe_o", "dv_z1", "dv_bn1", "ac_r1",
                     "dv_nm1", "ac_y1", "pe_y1t", "dv_y1T", "pe_f", "dv_y2",
                     "gp_zb", "gp_yc", "dv_bn2", "ac_r2", "dv_nm2", "ac_out", "drow",
                     "dv_sm", "ac_sm"]:
            sems[name] = es.enter_context(nc.semaphore(name))
        g = type("G", (), sems)
        dxb_s = [sems[f"dxb{i}"] for i in range(5)]
        dxr_s = [sems[f"dxr{i}"] for i in range(5)]
        dwq_s = [sems[f"dwq{i}"] for i in range(FB)]
        dwv_s = [sems[f"dwv{i}"] for i in range(FB)]
        dfw_s = [sems[f"dfw{i}"] for i in range(5)]
        dout_s = [sems[f"dout{i}"] for i in range(3)]

        def W(eng, *pairs):
            for sem, v in pairs:
                if v >= 1:
                    eng.wait_ge(sem, v)

        _smc = {"dv": 0, "ac": 0}
        def chain(eng, ins, which):
            sem = g.dv_sm if which == "dv" else g.ac_sm
            _smc[which] += 1
            ins.then_inc(sem, 1)
            eng.wait_ge(sem, _smc[which])

        qsT3 = s_qsT[:, :].rearrange("p (f n) -> p f n", f=FB)
        p_pa_bf = p_a[:, :].bitcast(BF16)[:, 0:1024]
        fw3 = s_fw[:, :].rearrange("p (ib n) -> p ib n", ib=FB)
        ptr3 = p_tr[:, :].rearrange("p (f n) -> p f n", f=FB)
        vb3 = [s_vb[i][:, :].rearrange("p (h c) -> p h c", h=FB) for i in range(3)]
        pb3 = p_b[:, :].rearrange("p (h c) -> p h c", h=FB)

        def ctx_off(hp):
            return (hp // 3) * 512 + (hp % 3) * 129

        def body_init(eng):
            for s in sems.values():
                eng.sem_clear(s)

        def body_sync(eng):
            eng.dma_start(s_id[:, :], id_h).then_inc(g.did, 16)
            for dst, src in [(s_sbv, bv_h), (s_bq, bq_h), (s_fsc, fsc_h)]:
                eng.dma_start(dst[:, :], src.unsqueeze(0).broadcast_to([128, D])
                              ).then_inc(g.dbi, 16)
            for j in range(NT):
                W(eng, (g.ac_out, j + 2))
                eng.dma_start(out_h[j * 128:(j + 1) * 128, :], s_of[j % 3][:, :]
                              ).then_inc(dout_s[j % 3], 16)
            for s in range(3):
                eng.wait_ge(dout_s[s], 16 * len([j for j in range(NT) if j % 3 == s]))

        def body_gp(eng):
            eng.dma_start(s_fbi_row[:, :], fbi_h.unsqueeze(0)).then_inc(g.drow, 16)
            eng.dma_start(s_bo[:, :], bo_h.unsqueeze(0).broadcast_to([128, D])
                          ).then_inc(g.dbo, 16)
            eng.dma_start(s_xb[0][:, :], x_h[0:128, :]).then_inc(dxb_s[0], 16)
            for kb in range(FB):
                eng.dma_start(s_wq[:, kb * D:(kb + 1) * D],
                              wq_h[kb * 128:(kb + 1) * 128, :]
                              ).then_inc(dwq_s[kb], 16)
            for j in range(1, 5):
                eng.dma_start(s_xb[j % 5][:, :], x_h[j * 128:(j + 1) * 128, :]
                              ).then_inc(dxb_s[j % 5], 16)
            for kb in range(FB):
                eng.dma_start(s_wv[:, kb * D:(kb + 1) * D],
                              wv_h[kb * 128:(kb + 1) * 128, :]
                              ).then_inc(dwv_s[kb], 16)
            eng.dma_start(s_wo[:, :].rearrange("p (kb n) -> p kb n", kb=FB),
                          wo_h.rearrange("(kb p) n -> p kb n", p=128)
                          ).then_inc(g.dwo, 16)
            for j in range(5, NT):
                W(eng, (g.pe_xt, j - 4))
                eng.dma_start(s_xb[j % 5][:, :], x_h[j * 128:(j + 1) * 128, :]
                              ).then_inc(dxb_s[j % 5], 16)
                if j == 8:
                    for ob in range(3):
                        eng.dma_start(s_fqb[ob][:, :],
                                      fw_h[ob * 128:(ob + 1) * 128, :]
                                      ).then_inc(dfw_s[ob], 16)
                if j == 10:
                    for ob in range(3, 5):
                        eng.dma_start(s_fqb[ob][:, :],
                                      fw_h[ob * 128:(ob + 1) * 128, :]
                                      ).then_inc(dfw_s[ob], 16)
            for ob in range(5, FB):
                W(eng, (g.dv_fgs, ob - 4))
                eng.dma_start(s_fqb[ob % 5][:, :],
                              fw_h[ob * 128:(ob + 1) * 128, :]
                              ).then_inc(dfw_s[ob % 5], 16)
            W(eng, (g.dv_ctx, 1))
            for j in range(4):
                eng.dma_start(s_xr[j][:, :], x_h[j * 128:(j + 1) * 128, :]
                              ).then_inc(dxr_s[j], 16)
            for k in range(NT):
                W(eng, (dxr_s[k % 5], 16 * (k // 5 + 1)), (g.dv_z1, k - 2))
                if k == 0:
                    W(eng, (g.dbo, 16))
                eng.tensor_tensor(s_zb[k % 3][:, :], s_xr[k % 5][:, :],
                                  s_bo[:, :], OP.add).then_inc(g.gp_zb, 1)
                if k + 4 < NT:
                    W(eng, (g.gp_zb, k))
                    eng.dma_start(s_xr[(k + 4) % 5][:, :],
                                  x_h[(k + 4) * 128:(k + 5) * 128, :]
                                  ).then_inc(dxr_s[(k + 4) % 5], 16)

        def body_pe(eng):
            W(eng, (g.did, 16), (dxb_s[0], 16))
            for f in range(FB):
                ins = eng.transpose(p_tr[:, f * 128:(f + 1) * 128],
                                    s_xb[0][:, f * 128:(f + 1) * 128], s_id[:, :])
            ins.then_inc(g.pe_xt, 1)
            for k in range(NT + 2):
                if k + 1 < NT:
                    c = k + 1
                    W(eng, (dxb_s[c % 5], 16 * (c // 5 + 1)), (g.dv_qt, k - 2),
                      (g.dv_xt, c))
                    for f in range(FB):
                        ins = eng.transpose(p_tr[:, f * 128:(f + 1) * 128],
                                            s_xb[c % 5][:, f * 128:(f + 1) * 128],
                                            s_id[:, :])
                    ins.then_inc(g.pe_xt, 1)
                if k >= 2:
                    c = k - 2
                    W(eng, (g.ac_ke, k - 1), (g.dv_v, k - 1))
                    if c == 0:
                        W(eng, (g.dv_pre, 2))
                    for hp in range(FB):
                        off = ctx_off(hp)
                        ins = eng.matmul(p_c[:, off:off + 129],
                                         s_ke[c % 3][:, hp * 128:(hp + 1) * 128],
                                         s_vb[c % 3][:, hp * 129:(hp + 1) * 129],
                                         start=(c == 0 and hp % 3 == 0),
                                         stop=(c == NT - 1 and hp in (2, 5, 7)),
                                         skip_group_check=True)
                    ins.then_inc(g.pe_cx, 1)
                if k < NT:
                    W(eng, (g.dv_xt, k + 1), (g.dv_q, k))
                    for kb in range(FB):
                        if k == 0:
                            W(eng, (dwq_s[kb], 16))
                        for hh in range(2):
                            ins = eng.matmul(
                                p_a[:, hh * 512:(hh + 1) * 512],
                                s_xT[k % 2][:, kb * 128:(kb + 1) * 128],
                                s_wq[:, kb * D + hh * 512: kb * D + (hh + 1) * 512],
                                start=(kb == 0), stop=(kb == FB - 1))
                    ins.then_inc(g.pe_q, 1)
                if k >= 2:
                    c = k - 2
                    W(eng, (g.dv_qs, k - 1), (g.dv_xt, min(k + 2, NT)),
                      (g.dv_qt, k - 2))
                    for f in range(FB):
                        ins = eng.transpose(p_tr[:, f * 128:(f + 1) * 128],
                                            s_qsb[c % 2][:, f * 128:(f + 1) * 128],
                                            s_id[:, :])
                    ins.then_inc(g.pe_qt, 1)
                if k < NT:
                    W(eng, (g.dv_v, k))
                    for kb in range(FB):
                        if k == 0:
                            W(eng, (dwv_s[kb], 16))
                        for hh in range(2):
                            ins = eng.matmul(
                                p_b[:, hh * 512:(hh + 1) * 512],
                                s_xT[k % 2][:, kb * 128:(kb + 1) * 128],
                                s_wv[:, kb * D + hh * 512: kb * D + (hh + 1) * 512],
                                start=(kb == 0), stop=(kb == FB - 1))
                    ins.then_inc(g.pe_v, 1)
            for ob in range(FB):
                W(eng, (g.dv_fgs, ob + 1), (g.dv_fwe, ob - 1))
                if ob == 0:
                    W(eng, (g.dv_qt, NT))
                if ob == 1:
                    W(eng, (g.dv_q, NT))
                bank = p_tr if ob % 2 == 0 else p_pa_bf
                for ib in range(FB):
                    ins = eng.transpose(bank[:, ib * 128:(ib + 1) * 128],
                                        s_fsg[ob % 2][:, ib * 128:(ib + 1) * 128],
                                        s_id[:, :])
                ins.then_inc(g.pe_fw, 1)
            for k in range(NT + 7):
                if k < NT:
                    W(eng, (g.ac_aT, k))
                    if k == 0:
                        W(eng, (g.dv_ctx, 1), (g.dv_fwe, FB))
                    for hp in range(FB):
                        ins = eng.matmul(
                            p_a[:, hp * 128:(hp + 1) * 128],
                            s_ctx[:, hp * 128:(hp + 1) * 128],
                            s_qsT[:, hp * N + k * 128: hp * N + (k + 1) * 128],
                            start=True, stop=True, skip_group_check=True)
                    ins.then_inc(g.pe_at, 1)
                if 4 <= k < NT + 4:
                    c = k - 4
                    W(eng, (g.ac_y1, k - 3), (g.dv_y1T, k - 4))
                    if k == 4:
                        W(eng, (g.dv_fwe, FB))
                    for f in range(FB):
                        ins = eng.transpose(p_tr[:, f * 128:(f + 1) * 128],
                                            s_y1b[c % 5][:, f * 128:(f + 1) * 128],
                                            s_id[:, :])
                    ins.then_inc(g.pe_y1t, 1)
                if 5 <= k < NT + 5:
                    c = k - 5
                    W(eng, (g.dv_y1T, k - 4), (g.dv_y2, k - 5))
                    if k == 5:
                        W(eng, (g.dv_ctx, 1), (g.dv_fwe, FB), (g.drow, 16),
                          (g.dv_pre, 2))
                    for kb in range(FB):
                        for hh in range(2):
                            ins = eng.matmul(
                                p_c[:, hh * 512:(hh + 1) * 512],
                                s_y1T[c % 2][:, kb * 128:(kb + 1) * 128],
                                s_fw[:, kb * D + hh * 512: kb * D + (hh + 1) * 512],
                                start=(kb == 0), stop=False)
                    for hh in range(2):
                        ins = eng.matmul(p_c[:, hh * 512:(hh + 1) * 512],
                                         s_one[0:1, 0:128],
                                         s_fbi_row[0:1, hh * 512:(hh + 1) * 512],
                                         start=False, stop=True)
                    ins.then_inc(g.pe_f, 1)
                if 1 <= k < NT + 1:
                    c = k - 1
                    W(eng, (g.ac_aT, k), (g.dv_z1, k - 1))
                    if k == 1:
                        W(eng, (g.dwo, 16), (g.dv_v, NT),
                          (g.drow, 16), (g.dv_pre, 2))
                    for kb in range(FB):
                        for hh in range(2):
                            ins = eng.matmul(
                                p_b[:, hh * 512:(hh + 1) * 512],
                                s_aT[c % 2][:, kb * 128:(kb + 1) * 128],
                                s_wo[:, kb * D + hh * 512: kb * D + (hh + 1) * 512],
                                start=(kb == 0), stop=(kb == FB - 1))
                    ins.then_inc(g.pe_o, 1)

        def body_dv(eng):
            eng.memset(s_cst[:, 0:1], EPS)
            eng.memset(s_cst[:, 1:2], -0.5)
            eng.memset(vb3[0][:, :, 128:129], 1.0)
            eng.memset(vb3[1][:, :, 128:129], 1.0)
            eng.memset(vb3[2][:, :, 128:129], 1.0)
            eng.memset(s_one[:, :], 1.0)
            for hp in range(FB):
                eng.memset(s_ctx[0:64, hp * 128 + 64: hp * 128 + 128], 0.0)
                eng.memset(s_ctx[64:128, hp * 128: hp * 128 + 64], 0.0)
            W(eng, (g.dbi, 48))
            eng.tensor_scalar(s_sbv[:, :], s_sbv[:, :], S_SCALE, None, OP.mult
                              ).then_inc(g.dv_pre, 2)
            eng.tensor_scalar(s_fsc[:, :], s_fsc[:, :], 2.0, None, OP.mult
                              ).then_inc(g.dv_pre, 2)

            W(eng, (g.pe_xt, 1))
            eng.tensor_copy(s_xT[0][:, :], p_tr[:, :]).then_inc(g.dv_xt, 1)
            for k in range(NT + 2):
                if k + 1 < NT:
                    c = k + 1
                    W(eng, (g.pe_xt, c + 1), (g.pe_v, k))
                    eng.tensor_copy(s_xT[c % 2][:, :], p_tr[:, :]).then_inc(g.dv_xt, 1)
                if 1 <= k <= NT:
                    c = k - 1
                    W(eng, (g.ac_e, k))
                    chain(eng, eng.tensor_reduce(
                        s_ss[c % 2][:, :],
                        s_ef[c % 2][:, :].rearrange("p (h w) -> p h w", h=H),
                        AX.X, OP.add), "dv")
                    chain(eng, eng.reciprocal(s_sr[c % 2][:, :], s_ss[c % 2][:, :]),
                          "dv")
                    W(eng, (g.ac_ke, k - 2), (g.pe_qt, k - 2))
                    eng.scalar_tensor_tensor(
                        s_qsb[c % 2][:, :].rearrange("p (h w) -> p h w", h=H),
                        s_ef[c % 2][:, :].rearrange("p (h w) -> p h w", h=H),
                        S_SCALE,
                        s_sr[c % 2][:, :].unsqueeze(2).broadcast_to([128, H, DH]),
                        OP.mult, OP.mult).then_inc(g.dv_qs, 1)
                if k < NT:
                    W(eng, (g.pe_q, k + 1), (g.ac_e, k - 1))
                    if k == 0:
                        W(eng, (g.dbi, 48))
                    for hh in range(2):
                        ins = eng.scalar_tensor_tensor(
                            s_qf[k % 2][:, hh * 512:(hh + 1) * 512],
                            p_a[:, hh * 512:(hh + 1) * 512],
                            1.0, s_bq[:, hh * 512:(hh + 1) * 512],
                            OP.mult, OP.add)
                    ins.then_inc(g.dv_q, 1)
                if k >= 2:
                    c = k - 2
                    W(eng, (g.pe_qt, k - 1))
                    eng.tensor_copy(qsT3[:, :, c * 128:(c + 1) * 128],
                                    ptr3[:, :, :]).then_inc(g.dv_qt, 1)
                if k < NT:
                    W(eng, (g.pe_v, k + 1), (g.pe_cx, k - 2))
                    eng.tensor_copy(vb3[k % 3][:, :, 0:128], pb3[:, :, :]).then_inc(g.dv_v, 1)
            for ob in range(FB):
                W(eng, (dfw_s[ob % 5], 16 * (ob // 5 + 1)))
                if ob == 0:
                    W(eng, (g.dbi, 48))
                W(eng, (g.pe_fw, ob - 1))
                eng.tensor_scalar(s_fsg[ob % 2][:, :], s_fqb[ob % 5][:, :],
                                  0.0, -0.5, OP.is_ge, OP.add
                                  ).then_inc(g.dv_fgs, 1)
                if ob >= 1:
                    W(eng, (g.pe_fw, ob))
                    bank = p_tr if (ob - 1) % 2 == 0 else p_pa_bf
                    eng.scalar_tensor_tensor(
                        fw3[:, :, (ob - 1) * 128: ob * 128],
                        bank[:, :].rearrange("p (i n) -> p i n", i=FB), 1.0,
                        s_fsc[:, (ob - 1) * 128: ob * 128].unsqueeze(1
                              ).broadcast_to([128, FB, 128]),
                        OP.mult, OP.mult).then_inc(g.dv_fwe, 1)
            W(eng, (g.pe_fw, FB))
            eng.scalar_tensor_tensor(
                fw3[:, :, (FB - 1) * 128: FB * 128],
                p_pa_bf[:, :].rearrange("p (i n) -> p i n", i=FB), 1.0,
                s_fsc[:, (FB - 1) * 128: FB * 128].unsqueeze(1
                      ).broadcast_to([128, FB, 128]),
                OP.mult, OP.mult).then_inc(g.dv_fwe, 1)
            W(eng, (g.pe_cx, NT))
            for hp in range(FB):
                off = ctx_off(hp)
                ins = eng.reciprocal(s_m[:, hp:hp + 1], p_c[:, off + 128:off + 129])
            chain(eng, ins, "dv")
            chain(eng, eng.tensor_scalar(s_m[:, :], s_m[:, :], S_SCALE, None,
                                         OP.mult), "dv")
            for hp in range(FB):
                off = ctx_off(hp)
                eng.scalar_tensor_tensor(
                    s_ctx[0:64, hp * 128: hp * 128 + 64],
                    p_c[0:64, off:off + 64],
                    s_m[0:64, hp:hp + 1],
                    s_sbv[0:64, hp * 128: hp * 128 + 64],
                    OP.mult, OP.add)
                ins = eng.scalar_tensor_tensor(
                    s_ctx[64:128, hp * 128 + 64: (hp + 1) * 128],
                    p_c[64:128, off + 64:off + 128],
                    s_m[64:128, hp:hp + 1],
                    s_sbv[64:128, hp * 128 + 64: (hp + 1) * 128],
                    OP.mult, OP.add)
            ins.then_inc(g.dv_ctx, 1)
            for k in range(NT + 7):
                if 2 <= k < NT + 2:
                    c = k - 2
                    st = s_st[c % 2]
                    W(eng, (g.pe_o, k - 1), (g.gp_zb, k - 1),
                      (g.ac_y1, k - 3))
                    for hh in range(2):
                        ins = eng.scalar_tensor_tensor(
                            s_z1[c % 2][:, hh * 512:(hh + 1) * 512],
                            p_b[:, hh * 512:(hh + 1) * 512],
                            1.0, s_zb[c % 3][:, hh * 512:(hh + 1) * 512],
                            OP.mult, OP.add)
                    ins.then_inc(g.dv_z1, 1)
                    W(eng, (g.dv_z1, k - 1))
                    eng.bn_stats(st[:, 0:6], s_z1[c % 2][:, 0:512])
                    chain(eng, eng.bn_stats(st[:, 6:12], s_z1[c % 2][:, 512:1024]),
                          "dv")
                    eng.bn_aggr(st[:, 12:14], st[:, 0:12]).then_inc(g.dv_bn1, 1)
                    W(eng, (g.ac_r1, k - 1))
                    eng.tensor_scalar(st[:, 16:17], st[:, 12:13], st[:, 15:16],
                                      -1.0, OP.mult, OP.mult).then_inc(g.dv_nm1, 1)
                if 5 <= k < NT + 5:
                    c = k - 5
                    W(eng, (g.pe_f, k - 4))
                    for hh in range(2):
                        ins = eng.tensor_scalar(
                            s_yt[c % 2][:, hh * 512:(hh + 1) * 512],
                            p_c[:, hh * 512:(hh + 1) * 512],
                            100.0, -100.0, OP.min, OP.max)
                    ins.then_inc(g.dv_y2, 1)
                if 6 <= k < NT + 6:
                    c = k - 6
                    st = s_st[c % 2]
                    W(eng, (g.ac_y1, k - 5), (g.ac_out, k - 7))
                    chain(eng, eng.scalar_tensor_tensor(
                        s_yb[c % 3][:, :], s_yt[c % 2][:, :], 1.0,
                        s_y1b[c % 5][:, :], OP.mult, OP.add), "dv")
                    eng.bn_stats(st[:, 24:30], s_yb[c % 3][:, 0:512])
                    chain(eng, eng.bn_stats(st[:, 30:36], s_yb[c % 3][:, 512:1024]),
                          "dv")
                    eng.bn_aggr(st[:, 17:19], st[:, 24:36]).then_inc(g.dv_bn2, 1)
                if 7 <= k < NT + 7:
                    c = k - 7
                    st = s_st[c % 2]
                    W(eng, (g.ac_r2, k - 6))
                    eng.tensor_scalar(st[:, 21:22], st[:, 17:18], st[:, 20:21],
                                      -1.0, OP.mult, OP.mult).then_inc(g.dv_nm2, 1)

        def body_ac(eng):

            for k in range(NT + 1):
                if k < NT:
                    W(eng, (g.dv_q, k + 1), (g.dv_qs, k - 1))
                    eng.activation(s_ef[k % 2][:, :], s_qf[k % 2][:, :], AF.Exp).then_inc(g.ac_e, 1)
                if k >= 1:
                    c = k - 1
                    W(eng, (g.dv_qs, k), (g.pe_cx, k - 3))
                    eng.activation(s_ke[c % 3][:, :], s_qsb[c % 2][:, :], AF.Exp).then_inc(g.ac_ke, 1)
            for k in range(NT + 7):
                if k < NT:
                    W(eng, (g.pe_at, k + 1), (g.pe_o, k - 1))
                    eng.activation(s_aT[k % 2][:, 0:512], p_a[:, 0:512], AF.Copy)
                    eng.activation(s_aT[k % 2][:, 512:1024], p_a[:, 512:1024],
                                   AF.Copy).then_inc(g.ac_aT, 1)
                if 4 <= k < NT + 4:
                    c = k - 4
                    W(eng, (g.pe_y1t, k - 3), (g.pe_f, k - 5))
                    eng.activation(s_y1T[c % 2][:, :], p_tr[:, :], AF.Copy
                                   ).then_inc(g.dv_y1T, 1)
                if 2 <= k < NT + 2:
                    c = k - 2
                    st = s_st[c % 2]
                    W(eng, (g.dv_bn1, k - 1))
                    if k == 2:
                        W(eng, (g.dv_pre, 2))
                    chain(eng, eng.activation(st[:, 14:15], st[:, 13:14], AF.Ln,
                                              bias=s_cst[:, 0:1]), "ac")
                    eng.activation(st[:, 15:16], st[:, 14:15], AF.Exp,
                                   scale=s_cst[:, 1:2]).then_inc(g.ac_r1, 1)
                    W(eng, (g.dv_nm1, k - 1), (g.pe_y1t, k - 6), (g.dv_bn2, k - 6))
                    eng.activation(s_y1b[c % 5][:, :], s_z1[c % 2][:, :],
                                   AF.Identity, scale=st[:, 15:16],
                                   bias=st[:, 16:17]).then_inc(g.ac_y1, 1)
                if 7 <= k < NT + 7:
                    c = k - 7
                    st = s_st[c % 2]
                    W(eng, (g.dv_bn2, k - 6))
                    chain(eng, eng.activation(st[:, 19:20], st[:, 18:19], AF.Ln,
                                              bias=s_cst[:, 0:1]), "ac")
                    eng.activation(st[:, 20:21], st[:, 19:20], AF.Exp,
                                   scale=s_cst[:, 1:2]).then_inc(g.ac_r2, 1)
                    W(eng, (g.dv_nm2, k - 6))
                    if c >= 3:
                        W(eng, (dout_s[c % 3], 16 * (c // 3)))
                    eng.activation(s_of[c % 3][:, :], s_yb[c % 3][:, :],
                                   AF.Identity, scale=st[:, 20:21],
                                   bias=st[:, 21:22]).then_inc(g.ac_out, 1)
            eng.activation(s_cst[:, 2:3], s_cst[:, 0:1], AF.Identity
                           ).then_inc(g.ac_out, 1)

        with nc.Block("init") as b0:
            b0.sync(body_init)
        with nc.Block("main") as blk:
            blk.sync(body_sync)
            blk.gpsimd(body_gp)
            blk.tensor(body_pe)
            blk.vector(body_dv)
            blk.scalar(body_ac)

        es_b.close()
    return nc


_CACHE = {}


def kernel(**inputs):
    _ = inputs.get("L0_attn_Wk"), inputs.get("L0_attn_bk")
    if "nc" not in _CACHE:
        _CACHE["nc"] = build_nc()
    nc = _CACHE["nc"]
    x = np.ascontiguousarray(np.asarray(inputs["L0_attn_x"], dtype=np.float32))
    common = {
        "wq": np.ascontiguousarray(np.asarray(inputs["L0_attn_Wq"], np.float32)),
        "wv": np.ascontiguousarray(np.asarray(inputs["L0_attn_Wv"], np.float32)),
        "wo": np.ascontiguousarray(np.asarray(inputs["L0_attn_Wo"], np.float32)),
        "fw": np.ascontiguousarray(np.asarray(inputs["L0_ffn_weights"], np.float32)),
        "bq": np.ascontiguousarray(np.asarray(inputs["L0_attn_bq"], np.float32)),
        "bv": np.ascontiguousarray(np.asarray(inputs["L0_attn_bv"], np.float32)),
        "bo": np.ascontiguousarray(np.asarray(inputs["L0_attn_bo"], np.float32)),
        "fsc": np.ascontiguousarray(np.asarray(inputs["L0_ffn_scale"], np.float32)),
        "fbi": np.ascontiguousarray(np.asarray(inputs["L0_ffn_bias"], np.float32)),
        "ident": np.eye(128, dtype=ml_dtypes.bfloat16),
    }
    in_maps = [dict(common, x=np.ascontiguousarray(x[c])) for c in range(8)]
    res = run_bass_kernel_spmd(nc, in_maps, core_ids=list(range(8)))
    return np.stack([np.asarray(r["out"], np.float32) for r in res.results])
